# version 26
# baseline (speedup 1.0000x reference)
import contextlib
import numpy as np
import ml_dtypes
import concourse.bass as bass
import concourse.mybir as mybir
from concourse.bass_utils import run_bass_kernel_spmd

F32 = mybir.dt.float32
F32R = mybir.dt.float32r
BF16 = mybir.dt.bfloat16
I32 = mybir.dt.int32
AF = mybir.ActivationFunctionType
ALU = mybir.AluOpType

S, D, H, HD, W = 16384, 1024, 16, 64, 512
NCORE = 8
C = S // W
OWN = C // NCORE
NCI = OWN + 1
NIC = D // 128
TT = W
NTT = W // 128
QKVC = 3 * D
MASKV = -30000.0
INV1024 = 1.0 / 1024.0


class _St:
    def __init__(self, name, dma):
        self.name, self.dma = name, dma
        self.ops = []
        self.cnt = 0
        self.hw = {}


class _Bld:
    def __init__(self):
        self.st = {n: _St(n, n in ("sync", "gpsimd")) for n in
                   ("tensor", "vector", "scalar", "sync", "gpsimd")}

    def op(self, eng, fn, wait=(), mark=False):
        s = self.st[eng]
        ws = []
        for w in wait:
            if w is None:
                continue
            ps, v = w
            if ps is s or v <= 0:
                continue
            if s.hw.get(ps.name, -1) >= v:
                continue
            s.hw[ps.name] = v
            ws.append((ps, v))
        inc = 16 if s.dma else (1 if mark else 0)
        s.ops.append((ws, fn, inc))
        if inc:
            s.cnt += inc
        return (s, s.cnt)

    def ev(self, eng):
        s = self.st[eng]
        return (s, s.cnt)


def build_nc(taps=False):
    nc = bass.Bass("TRN2", target_bir_lowering=False)
    b = _Bld()
    T, V, A, SY, G = "tensor", "vector", "scalar", "sync", "gpsimd"

    xin = nc.dram_tensor("xin", [NIC, 128, NCI * TT], F32, kind="ExternalInput").ap()
    wqkv = nc.dram_tensor("wqkv", [NIC, 128, QKVC], F32, kind="ExternalInput").ap()
    wout = nc.dram_tensor("wout", [NIC, 128, D], F32, kind="ExternalInput").ap()
    tern = nc.dram_tensor("tern", [NIC, 128, D], I32, kind="ExternalInput").ap()
    bias8 = nc.dram_tensor("bias8", [128, NIC], F32, kind="ExternalInput").ap()
    scal = nc.dram_tensor("scal", [1, 1], F32, kind="ExternalInput").ap()
    c_tri = nc.dram_tensor("c_tri", [128, 128], F32, kind="ExternalInput").ap()
    c_prev = nc.dram_tensor("c_prev", [128, TT], F32, kind="ExternalInput").ap()
    c_ones_r = nc.dram_tensor("c_ones_r", [128, 1], F32R, kind="ExternalInput").ap()
    c_ones_row = nc.dram_tensor("c_ones_row", [1, 128], F32, kind="ExternalInput").ap()
    xout = nc.dram_tensor("xout", [NIC, 128, OWN * TT], F32, kind="ExternalOutput").ap()

    s_wqh = nc.dram_tensor("s_wqh", [NIC, 128, QKVC], BF16).ap()
    s_wql = nc.dram_tensor("s_wql", [NIC, 128, QKVC], BF16).ap()
    s_woh = nc.dram_tensor("s_woh", [NIC, 128, D], BF16).ap()
    s_wol = nc.dram_tensor("s_wol", [NIC, 128, D], BF16).ap()
    s_tern = nc.dram_tensor("s_tern", [NIC, 128, D], BF16).ap()

    if taps:
        tp_rbc = nc.dram_tensor("tp_rbc", [128, 512], F32, kind="ExternalOutput").ap()
        tp_s0 = nc.dram_tensor("tp_s0", [128, 512], F32, kind="ExternalOutput").ap()
        tp_row = nc.dram_tensor("tp_row", [1, 512], F32, kind="ExternalOutput").ap()
        tp_x1 = nc.dram_tensor("tp_x1", [NIC, 128, NCI * TT], F32, kind="ExternalOutput").ap()
        tp_x2 = nc.dram_tensor("tp_x2", [NIC, 128, NCI * TT], F32, kind="ExternalOutput").ap()

    ctx = contextlib.ExitStack()
    sbt = lambda name, shape, dt: ctx.enter_context(nc.sbuf_tensor(name, shape, dt))
    psb = lambda name: ctx.enter_context(nc.psum_tensor(name, [128, 512], F32))

    with ctx:
        xb = [sbt(f"xb{i}", [128, NIC, TT], F32) for i in range(2)]
        xq = sbt("xq", [128, NIC, TT], BF16)
        s0 = sbt("s0", [128, 512], F32)
        s1 = sbt("s1", [128, 512], F32)
        i0 = sbt("i0", [128, 512], I32)
        i1 = sbt("i1", [128, 512], I32)
        g0 = sbt("g0", [128, 512], F32)
        g1 = sbt("g1", [128, 512], F32)
        srr = sbt("srr", [128, 512], F32R)
        sqr = sbt("sqr", [128, 512], F32R)
        ahi = sbt("ahi", [128, NIC, TT], BF16)
        alo = sbt("alo", [128, NIC, TT], BF16)
        qh = sbt("qh", [128, NIC, TT], BF16)
        ql = sbt("ql", [128, NIC, TT], BF16)
        kh = [sbt(f"kh{i}", [128, NIC, TT], BF16) for i in range(2)]
        kl = [sbt(f"kl{i}", [128, NIC, TT], BF16) for i in range(2)]
        vt = [sbt(f"vt{i}", [128, NTT, H * 65], F32) for i in range(2)]
        wbh = [sbt(f"wbh{i}", [128, NIC, 512], BF16) for i in range(2)]
        wbl = [sbt(f"wbl{i}", [128, NIC, 512], BF16) for i in range(2)]
        rows = sbt("rows", [1, 8 * 512], F32)
        rbc = sbt("rbc", [128, 512], F32)
        tri = sbt("tri", [128, 128], F32)
        prevm = sbt("prevm", [128, TT], F32)
        onesr = sbt("onesr", [128, 1], F32R)
        onesrow = sbt("onesrow", [1, 128], F32)
        hb = sbt("hb", [128, NIC], F32)
        hsbc = sbt("hsbc", [128, 1], F32)
        sc_r = sbt("sc_r", [1, 4], F32)

        PS = [psb(f"ps{i}") for i in range(8)]

        R_MEAN = rows[0:1, 0 * 512:1 * 512]
        R_MS = rows[0:1, 1 * 512:2 * 512]
        R_U = rows[0:1, 2 * 512:3 * 512]
        R_T1 = rows[0:1, 3 * 512:4 * 512]
        R_T2 = rows[0:1, 4 * 512:5 * 512]
        R_LA = rows[0:1, 5 * 512:6 * 512]
        R_LB = rows[0:1, 6 * 512:7 * 512]

        Pt = [[g0[:, :], g1[:, :]],
              [i0[:, :].bitcast(F32), i1[:, :].bitcast(F32)]]
        Pr = [srr, sqr]

        bfree = {i: None for i in range(8)}
        rows_tread = [None]
        s_free = [None, None]
        sq_scr_free = [None, None]

        def dma(eng, out_ap, in_ap, wait=()):
            return b.op(eng, lambda e, o=out_ap, i=in_ap: e.dma_start(out=o, in_=i),
                        wait=wait)

        def mm(out_ap, lh, rh, start, stop, wait=(), mark=False, tp=None):
            def f(e, o=out_ap, l=lh, r=rh, st=start, sp=stop, t=tp):
                return e.matmul(o, l, r, start=st, stop=sp, tile_position=t)
            return b.op(T, f, wait=wait, mark=mark)

        def vop(fn, wait=()):
            return b.op(V, fn, wait=wait, mark=True)

        def aop(fn, wait=()):
            return b.op(A, fn, wait=wait, mark=True)

        dma(SY, tri[:, :], c_tri)
        dma(SY, prevm[:, :], c_prev)
        dma(SY, onesr[:, :], c_ones_r)
        dma(SY, onesrow[:, :], c_ones_row)
        dma(SY, hb[:, :], bias8)
        e_cst = dma(SY, sc_r[0:1, 0:1], scal)
        vop(lambda e: e.tensor_scalar_mul(hb[:, :], hb[:, :], 0.5), wait=[e_cst])
        e_sm = vop(lambda e: e.tensor_scalar_max(sc_r[0:1, 0:1], sc_r[0:1, 0:1], 1e-5),
                   wait=[e_cst])
        vop(lambda e: e.reciprocal(sc_r[0:1, 1:2], sc_r[0:1, 0:1]), wait=[e_sm])
        e_hs = vop(lambda e: e.tensor_scalar_mul(sc_r[0:1, 2:3], sc_r[0:1, 0:1], 0.5),
                   wait=[e_sm])
        e_m = mm(PS[0][:, 0:1], onesrow[:, :], sc_r[0:1, 2:3], True, True,
                 wait=[e_hs, e_cst], mark=True)
        vop(lambda e: e.tensor_copy(hsbc[:, :], PS[0][:, 0:1]), wait=[e_m])

        e_prev = None
        for sl in range(2):
            colsl = slice(sl * 512, (sl + 1) * 512)
            e1 = dma(G, kh[0][:, :, :],
                     tern[:, :, colsl].rearrange("c p o -> p c o"),
                     wait=[e_prev] if e_prev else ())
            e2 = vop(lambda e: e.tensor_scalar_sub(kl[0][:, :, :],
                                                   kh[0][:, :, :], 1.0), wait=[e1])
            e_prev = dma(G, s_tern[:, :, colsl].rearrange("c p o -> p c o"),
                         kl[0][:, :, :], wait=[e2])
        e_tern_ready = e_prev
        pro_free = [None, None, e_tern_ready, None, None]

        pro_scr = [vvt[:, :, :].rearrange("p a b -> p (a b)")[:, 0:4096]
                   .rearrange("p (c t) -> p c t", c=NIC) for vvt in vt]

        e_wq_sl = [None] * 6
        e_wo_sl = [None] * 2
        xb_free = [None, None]
        wb_free = [None, None]
        pro_i = [0]

        def split_slices(tasks, sps=None):
            stg = [(kh[0], kl[0]), (kh[1], kl[1]), (qh, ql)]
            last = None
            loads = []
            i0 = pro_i[0]
            for ti, (src, dsth, dstl, sl, rl) in enumerate(tasks):
                i = i0 + ti
                par = i % 2
                colsl = slice(sl * 512, (sl + 1) * 512)
                loads.append(dma(SY, pro_scr[par],
                                 src[:, :, colsl].rearrange("c p t -> p c t"),
                                 wait=[pro_free[par]]))
            for ti, (src, dsth, dstl, sl, rl) in enumerate(tasks):
                i = pro_i[0]
                pro_i[0] += 1
                par = i % 2
                sp = (i % 3) if sps is None else sps[ti]
                colsl = slice(sl * 512, (sl + 1) * 512)
                scr = pro_scr[par]
                wh, wl = stg[sp]
                eh = aop(lambda e, s_=scr, w=wh: e.copy(w[:, :, :], s_),
                         wait=[loads[ti], pro_free[2 + sp]])
                el = vop(lambda e, s_=scr, w=wl, h=wh: e.tensor_tensor(
                    w[:, :, :], s_, h[:, :, :], ALU.subtract), wait=[eh])
                pro_free[par] = el
                e2 = dma(SY, dsth[:, :, colsl].rearrange("c p o -> p c o"),
                         wh[:, :, :], wait=[eh])
                e3 = dma(SY, dstl[:, :, colsl].rearrange("c p o -> p c o"),
                         wl[:, :, :], wait=[el])
                pro_free[2 + sp] = e3
                rl[sl] = e3
                last = e3
            return last

        def late_prologue_a1():
            return split_slices([(wqkv, s_wqh, s_wql, sl, e_wq_sl)
                                 for sl in (2, 3)])

        def late_prologue_a2():
            return split_slices([(wqkv, s_wqh, s_wql, sl, e_wq_sl)
                                 for sl in (4, 5)])

        def late_prologue_b():
            split_slices([(wqkv, s_wqh, s_wql, sl, e_wq_sl)
                          for sl in (0, 1)], sps=(1, 2))
            e_wo_ready = split_slices([(wout, s_woh, s_wol, sl, e_wo_sl)
                                       for sl in (0, 1)], sps=(1, 2))
            for i in range(2):
                vop(lambda e, vv=vt[i]: e.memset(
                    vv[:, :, :].rearrange("p a (h d) -> p a h d", h=H)
                    [:, :, :, 64:65], 1.0))
            return e_wo_ready
        evac_free = [None, None]

        x_load_ev, x_buf_of = [], []

        def load_chunk(ci):
            bi = ci % 2
            e = dma(SY, xb[bi][:, :, :],
                    xin[:, :, ci * TT:(ci + 1) * TT].rearrange("c p t -> p c t"),
                    wait=[xb_free[bi]])
            x_load_ev.append(e)
            x_buf_of.append(bi)
            return e

        def ln_stats(xbuf, x_ready, inv_seg=None):
            if not isinstance(x_ready, list):
                x_ready = [x_ready] * NIC
            """mean -> R_MEAN, invstd -> R_T1 (rows on partition 0).
            sum(x) via direct fp32 matmuls on xbuf; sum(x^2) via ACT Square
            into s0/s1 (parity) then fp32 matmuls - full fp32 statistics."""
            e_sum = e_sq = None
            onesf = onesr[:, :].bitcast(F32)
            for ic in range(NIC):
                sq = srr if ic % 2 == 0 else sqr
                e_sum = mm(PS[2][0:1, :], onesf, xbuf[:, ic, :],
                           ic == 0, ic == NIC - 1,
                           wait=[x_ready[ic], bfree[2]] if ic == 0
                           else [x_ready[ic]], mark=True)
                eq = aop(lambda e, i=ic, s=sq: e.activation(
                    s[:, :], xbuf[:, i, :], AF.Square),
                    wait=[x_ready[ic], sq_scr_free[ic % 2]])
                e_sq = mm(PS[3][0:1, :], onesr[:, :], sq[:, :],
                          ic == 0, ic == NIC - 1,
                          wait=[eq, bfree[3]] if ic == 0 else [eq], mark=True)
                sq_scr_free[ic % 2] = e_sq
            e1 = vop(lambda e: e.tensor_scalar_mul(R_MEAN, PS[2][0:1, :], INV1024),
                     wait=[e_sum, rows_tread[0], attention.rows_dma])
            bfree[2] = e1
            e2 = vop(lambda e: e.tensor_scalar_mul(R_MS, PS[3][0:1, :], INV1024),
                     wait=[e_sq])
            bfree[3] = e2
            e3 = vop(lambda e: e.tensor_mul(R_T1, R_MEAN, R_MEAN), wait=[e1])
            e5 = vop(lambda e: e.scalar_tensor_tensor(R_U, R_MS, 1e-5, R_T1,
                                                      ALU.add, ALU.subtract),
                     wait=[e2, e3])
            e6 = aop(lambda e: e.activation(R_T1, R_U, AF.Sqrt), wait=[e5])
            e7 = vop(lambda e: e.reciprocal(R_T2, R_T1), wait=[e6])
            e8 = vop(lambda e: e.tensor_mul(R_T1, R_T2, R_T2), wait=[e7])
            e9 = vop(lambda e: e.tensor_mul(R_T1, R_U, R_T1), wait=[e8])
            ea = vop(lambda e: e.tensor_scalar(R_T1, R_T1, -0.5, 1.5,
                                               ALU.mult, ALU.add), wait=[e9])
            out = inv_seg if inv_seg is not None else R_T1
            return vop(lambda e, o=out: e.tensor_mul(o, R_T2, R_T1), wait=[ea])

        def bcast_rows(arow, brow, wait):
            eA = mm(PS[0][:, :], onesrow[:, :], arow, True, True,
                    wait=list(wait) + [bfree[0]], mark=True)
            eB = mm(PS[1][:, :], onesrow[:, :], brow, True, True,
                    wait=list(wait) + [bfree[1]], mark=True)
            rows_tread[0] = eB
            return eA, eB

        def ffn_pre_a(xbuf, x_ready):
            if not isinstance(x_ready, list):
                x_ready = [x_ready] * NIC
            e_inv = ln_stats(xbuf, x_ready)
            e_al = vop(lambda e: e.tensor_scalar_mul(R_T2, R_T1, sc_r[0:1, 1:2]),
                       wait=[e_inv])
            e_b1 = vop(lambda e: e.scalar_tensor_tensor(
                R_MS, R_MEAN, -1.0, R_T2, ALU.mult, ALU.mult), wait=[e_al])
            e_b2 = vop(lambda e: e.tensor_scalar_add(R_MS, R_MS, 0.5), wait=[e_b1])
            return (xbuf, x_ready, e_b2)

        def ffn_pre_b(parta):
            xbuf, x_ready, e_b2 = parta
            eA, eB = bcast_rows(R_T2, R_MS, [e_b2])
            e_q = None

            def q_head(ic):
                sA, iA = (s0, i0) if ic % 2 == 0 else (s1, i1)
                e1 = vop(lambda e, i=ic, s=sA: e.tensor_tensor(
                    s[:, :], xbuf[:, i, :], PS[0][:, :], ALU.mult),
                    wait=[eA, x_ready[ic]])
                bfree[0] = e1
                e2 = vop(lambda e, s=sA: e.tensor_tensor(
                    s[:, :], s[:, :], PS[1][:, :], ALU.add), wait=[eB, e1])
                bfree[1] = e2
                return aop(lambda e, s=sA, ii=iA: e.copy(ii[:, :], s[:, :]),
                           wait=[e2])

            def q_tail(ic, e3):
                sA, iA, gA = (s0, i0, g0) if ic % 2 == 0 else (s1, i1, g1)
                e4 = vop(lambda e, s=sA, ii=iA, g=gA: e.tensor_tensor(
                    g[:, :], ii[:, :], s[:, :], ALU.is_gt), wait=[e3])
                e5 = vop(lambda e, s=sA, ii=iA, g=gA: e.tensor_tensor(
                    s[:, :], ii[:, :], g[:, :], ALU.subtract), wait=[e4])
                e6 = vop(lambda e, i=ic, s=sA: e.tensor_scalar(
                    xq[:, i, :], s[:, :], 127.0, -127.0, ALU.min, ALU.max),
                    wait=[e5])
                s_free[ic % 2] = e6
                return e6

            pend = []
            e_q_ic = [None] * NIC
            for ic in range(NIC):
                pend.append((ic, q_head(ic)))
                if len(pend) == 2:
                    ic0, e3 = pend.pop(0)
                    e_q_ic[ic0] = q_tail(ic0, e3)
            while pend:
                ic0, e3 = pend.pop(0)
                e_q_ic[ic0] = q_tail(ic0, e3)
            return e_q_ic

        def ffn_mm(xbuf, e_q_ic):
            res_ic = [None] * NIC
            e_res = None
            for sl in range(2):
                wi = sl % 2
                e_w = dma(G, wbh[wi][:, :, :],
                          s_tern[:, :, sl * 512:(sl + 1) * 512]
                          .rearrange("c p o -> p c o"),
                          wait=[e_tern_ready, wb_free[wi]])
                em = None
                for j in range(4):
                    oc = sl * 4 + j
                    bank = PS[4 + (oc % 4)]
                    bw = [e_w, bfree[4 + (oc % 4)]]
                    for ic in range(NIC):
                        em = mm(bank[:, :], wbh[wi][:, ic, j * 128:(j + 1) * 128],
                                xq[:, ic, :], ic == 0, ic == NIC - 1,
                                wait=(bw + [e_q_ic[ic]]) if ic == 0
                                else [e_q_ic[ic]], mark=(ic == NIC - 1))
                    gg = g0 if oc % 2 == 0 else g1
                    et = aop(lambda e, bk=bank, o=oc, g=gg: e.activation(
                        g[:, :], bk[:, :], AF.Identity,
                        bias=hb[:, o:o + 1], scale=hsbc[:, 0:1]),
                        wait=[em, evac_free[oc % 2]])
                    bfree[4 + (oc % 4)] = et
                    e_res = vop(lambda e, o=oc, g=gg: e.tensor_tensor(
                        xbuf[:, o, :], xbuf[:, o, :], g[:, :], ALU.add), wait=[et])
                    evac_free[oc % 2] = e_res
                    res_ic[oc] = e_res
                wb_free[wi] = em
            return res_ic

        def attn_ln_a(xbuf, x_ready):
            if not isinstance(x_ready, list):
                x_ready = [x_ready] * NIC
            e_inv = ln_stats(xbuf, x_ready, inv_seg=R_LA)
            e_b1 = vop(lambda e: e.scalar_tensor_tensor(
                R_LB, R_MEAN, -1.0, R_LA, ALU.mult, ALU.mult), wait=[e_inv])
            return (xbuf, x_ready, e_b1)

        def attn_ln_b(parta):
            xbuf, x_ready, e_b1 = parta
            eA, eB = bcast_rows(R_LA, R_LB, [e_b1])
            e_last = None
            ln_ic = [None] * NIC
            for ic in range(NIC):
                sA = s0 if ic % 2 == 0 else s1
                e1 = vop(lambda e, i=ic, s=sA: e.tensor_tensor(
                    s[:, :], xbuf[:, i, :], PS[0][:, :], ALU.mult),
                    wait=[eA, x_ready[ic]])
                bfree[0] = e1
                e2 = vop(lambda e, s=sA: e.tensor_tensor(
                    s[:, :], s[:, :], PS[1][:, :], ALU.add), wait=[eB, e1])
                bfree[1] = e2
                e3 = vop(lambda e, i=ic, s=sA: e.tensor_copy(ahi[:, i, :], s[:, :]),
                         wait=[e2])
                ln_ic[ic] = vop(lambda e, i=ic, s=sA: e.tensor_tensor(
                    alo[:, i, :], s[:, :], ahi[:, i, :], ALU.subtract), wait=[e3])
                s_free[ic % 2] = ln_ic[ic]
            return ln_ic

        def qkv(ci, kv_par, ln_ic):
            halo = (ci == 0)
            e_done = None
            for sl in range(6):
                wi = sl % 2
                cols = slice(sl * 512, (sl + 1) * 512)
                is_q = sl < 2
                is_k = 2 <= sl < 4
                if is_q and halo:
                    continue
                e_wh = dma(G, wbh[wi][:, :, :],
                           s_wqh[:, :, cols].rearrange("c p o -> p c o"),
                           wait=[e_wq_sl[sl], wb_free[wi]])
                e_wl = dma(G, wbl[wi][:, :, :],
                           s_wql[:, :, cols].rearrange("c p o -> p c o"),
                           wait=[e_wq_sl[sl], wb_free[wi]])
                em_last = None
                if is_q or is_k:
                    dsth = qh if is_q else kh[kv_par]
                    dstl = ql if is_q else kl[kv_par]
                    base = (sl - (0 if is_q else 2)) * 4
                    for j in range(4):
                        fc = base + j
                        bank = PS[4 + (fc % 4)]
                        first = True
                        em = None
                        for ic in range(NIC):
                            for (wt, at_) in ((wbh[wi], ahi), (wbh[wi], alo),
                                              (wbl[wi], ahi)):
                                last = (ic == NIC - 1) and (wt is wbl[wi])
                                em = mm(bank[:, :],
                                        wt[:, ic, j * 128:(j + 1) * 128],
                                        at_[:, ic, :], first, last,
                                        wait=[e_wh, e_wl, ln_ic[ic],
                                              bfree[4 + (fc % 4)]]
                                        if first else [ln_ic[ic]], mark=last)
                                first = False
                        sfence = []
                        if fc == base:
                            if ci == 0 and is_k:
                                sfence = [pro_free[2]]
                            elif ci == 1:
                                sfence = ([pro_free[3]] if is_k
                                          else [pro_free[4]])
                        eh = vop(lambda e, f=fc, bk=bank, d=dsth: e.tensor_copy(
                            d[:, f, :], bk[:, :]), wait=[em] + sfence)
                        e_done = vop(lambda e, f=fc, bk=bank, dh=dsth, dl=dstl:
                                     e.tensor_tensor(dl[:, f, :], bk[:, :],
                                                     dh[:, f, :], ALU.subtract),
                                     wait=[eh])
                        bfree[4 + (fc % 4)] = e_done
                        em_last = em
                else:
                    vcol = slice((sl - 4) * 512, (sl - 3) * 512)
                    for tt in range(NTT):
                        bank = PS[4 + (tt % 4)]
                        first = True
                        em = None
                        for ic in range(NIC):
                            for (at_, wt) in ((ahi, wbh[wi]), (alo, wbh[wi]),
                                              (ahi, wbl[wi])):
                                last = (ic == NIC - 1) and (wt is wbl[wi])
                                em = mm(bank[:, :],
                                        at_[:, ic, tt * 128:(tt + 1) * 128],
                                        wt[:, ic, :], first, last,
                                        wait=[e_wh, e_wl, ln_ic[ic],
                                              bfree[4 + (tt % 4)]]
                                        if first else [ln_ic[ic]], mark=last)
                                first = False
                        h0 = (sl - 4) * 8
                        e_done = vop(lambda e, t=tt, bk=bank, hh=h0:
                                     e.tensor_copy(
                                         vt[kv_par][:, t, :]
                                         .rearrange("p (h d) -> p h d", h=H)
                                         [:, hh:hh + 8, 0:64],
                                         bk[:, :].rearrange("p (h d) -> p h d",
                                                            h=8)),
                                     wait=[em])
                        bfree[4 + (tt % 4)] = e_done
                        em_last = em
                wb_free[wi] = em_last
            return e_done

        def attention(ci, kv_par, e_kv_cur, e_kv_prev):
            first_local = (ci == 1)
            e_out = None
            pv_ev = attention.pv_ev
            i1f = i1[:, :].bitcast(F32)
            pq = []
            for p in range(8):
                em_O = [None, None]
                for j in range(8):
                    khb = kh[1 - kv_par] if j < 4 else kh[kv_par]
                    klb = kl[1 - kv_par] if j < 4 else kl[kv_par]
                    vbuf = vt[1 - kv_par] if j < 4 else vt[kv_par]
                    jj = j % 4
                    qlo = 128 * (j - 4) if j > 4 else 0
                    par = j % 2
                    banks = (PS[4 + par], PS[6 + par])
                    em_s = [None, None]
                    for ih in (0, 1):
                        bank = banks[ih]
                        pr = slice(64 * ih, 64 * ih + 64)
                        first = True
                        for (kk, qq) in ((khb, qh), (khb, ql), (klb, qh)):
                            em_s[ih] = mm(bank[:, qlo:512],
                                          kk[pr, p, jj * 128:(jj + 1) * 128],
                                          qq[pr, p, qlo:512], first, kk is klb,
                                          wait=[e_kv_cur,
                                                e_kv_prev if j < 4 else None,
                                                bfree[4 + 2 * ih + par]]
                                          if first else (),
                                          mark=(kk is klb), tp=(64 * ih, 0))
                            first = False
                    mevs = [[em_s[0]], [em_s[1]]]
                    for ih in (0, 1):
                        bank = banks[ih]
                        if j < 4 and first_local:
                            mevs[ih].append(vop(
                                lambda e, bk=bank: e.tensor_tensor(
                                    bk[:, :], bk[:, :], prevm[:, :], ALU.add),
                                wait=[em_s[ih]]))
                        if j >= 4:
                            qs = slice(128 * (j - 4), 128 * (j - 4) + 128)
                            mevs[ih].append(vop(
                                lambda e, bk=bank, q=qs: e.tensor_tensor(
                                    bk[:, q], bk[:, q], tri[:, :], ALU.add),
                                wait=[em_s[ih]]))
                    if j == 4:
                        finish_pair()
                    if len(pq) == 2:
                        emit_tail(pq.pop(0))
                    ee = [None, None]
                    for ih in (0, 1):
                        P = Pt[ih][par]
                        xw = [attention.i1_free] if (ih == 1 and par == 1) else []
                        ee[ih] = aop(lambda e, bk=banks[ih], P_=P, q0=qlo:
                                     e.activation(P_[:, q0:512], bk[:, q0:512],
                                                  AF.Exp, scale=0.125),
                                     wait=mevs[ih] + [pv_ev[ih][par]] + xw)
                        bfree[4 + 2 * ih + par] = ee[ih]
                    pq.append((p, em_O, j, ee, qlo, jj, vbuf))
                while pq:
                    emit_tail(pq.pop(0))
                e_oA = vop(lambda e: e.tensor_copy(s0[0:64, :], PS[3][0:64, :]),
                           wait=[em_O[0], s_free[0]])
                e_sA = vop(lambda e: e.tensor_copy(s1[64:65, :], PS[3][64:65, :]),
                           wait=[em_O[0], attention.s1_free])
                bfree[3] = e_sA
                e_oB = vop(lambda e: e.tensor_copy(s1[0:64, :], PS[1][0:64, :]),
                           wait=[em_O[1], attention.s1_free])
                e_sB = vop(lambda e: e.tensor_copy(i1f[64:65, :],
                                                   PS[1][64:65, :]),
                           wait=[em_O[1], attention.i1_free])
                bfree[1] = e_sB
                d_oB = dma(SY, s0[64:128, :], s1[0:64, :],
                           wait=[e_oB, s_free[0]])
                attention.s1_free = d_oB
                d_sA = dma(SY, R_MEAN, s1[64:65, :],
                           wait=[e_sA, rows_tread[0], attention.rows_dma])
                attention.s1_free = d_sA
                d_sB = dma(SY, R_MS, i1f[64:65, :],
                           wait=[e_sB, rows_tread[0]])
                attention.rows_dma = d_sB
                attention.i1_free = d_sB
                attention.pending = (p, d_sA, d_sB, d_oB, e_oA)
            finish_pair()
            return attention.last_out

        def finish_pair():
            if attention.pending is None:
                return
            p, d_sA, d_sB, d_oB, e_oA = attention.pending
            attention.pending = None
            e_bcA = mm(PS[0][0:64, :], onesrow[0:1, 0:64], R_MEAN, True, True,
                       wait=[d_sA, bfree[0]], mark=True)
            e_bc = mm(PS[0][64:128, :], onesrow[0:1, 0:64], R_MS, True, True,
                      wait=[d_sB], mark=True)
            rows_tread[0] = e_bc
            e_cp = vop(lambda e: e.reciprocal(rbc[:, :], PS[0][:, :]),
                       wait=[e_bcA, e_bc])
            bfree[0] = e_cp
            e_n = vop(lambda e: e.tensor_tensor(s0[:, :], s0[:, :],
                                                rbc[:, :], ALU.mult),
                      wait=[e_cp, e_oA, d_oB])
            e_h = vop(lambda e, pp=p: e.tensor_copy(ahi[:, pp, :], s0[:, :]),
                      wait=[e_n])
            e_out = vop(lambda e, pp=p: e.tensor_tensor(
                alo[:, pp, :], s0[:, :], ahi[:, pp, :], ALU.subtract),
                wait=[e_h])
            s_free[0] = e_out
            attention.last_out = e_out
            if taps and attention.dbg:
                attention.dbg = False
                dma(SY, tp_rbc, rbc[:, :], wait=[e_out])
                dma(SY, tp_s0, s0[:, :], wait=[e_out])
                dma(SY, tp_row, R_MEAN, wait=[e_out])


        attention.pv_ev = [[None, None], [None, None]]
        attention.s1_free = None
        attention.rows_dma = None
        attention.i1_free = None
        attention.pending = None
        attention.last_out = None
        attention.dbg = False

        def emit_tail(pend):
            p, em_O, j, ee, qlo, jj, vbuf = pend
            par = j % 2
            for ih in (0, 1):
                head = 2 * p + ih
                obank = PS[3] if ih == 0 else PS[1]
                em_O[ih] = mm(obank[0:65, qlo:512],
                              vbuf[:, jj, head * 65:head * 65 + 65],
                              Pt[ih][par][:, qlo:512], j == 0, j == 7,
                              wait=[ee[ih], bfree[3 if ih == 0 else 1]]
                              if j == 0 else [ee[ih]],
                              mark=True)
                attention.pv_ev[ih][par] = em_O[ih]

        def outproj(xbuf, e_at):
            res_ic = [None] * NIC
            for sl in range(2):
                wi = sl % 2
                cols = slice(sl * 512, (sl + 1) * 512)
                e_wh = dma(G, wbh[wi][:, :, :],
                           s_woh[:, :, cols].rearrange("c p o -> p c o"),
                           wait=[e_wo_sl[sl], wb_free[wi]])
                e_wl = dma(G, wbl[wi][:, :, :],
                           s_wol[:, :, cols].rearrange("c p o -> p c o"),
                           wait=[e_wo_sl[sl], wb_free[wi]])
                em = None
                for j in range(4):
                    oc = sl * 4 + j
                    bank = PS[4 + (oc % 4)]
                    first = True
                    for ic in range(NIC):
                        for (wt, at_) in ((wbh[wi], ahi), (wbh[wi], alo),
                                          (wbl[wi], ahi)):
                            last = (ic == NIC - 1) and (wt is wbl[wi])
                            em = mm(bank[:, :], wt[:, ic, j * 128:(j + 1) * 128],
                                    at_[:, ic, :], first, last,
                                    wait=[e_wh, e_wl, e_at,
                                          bfree[4 + (oc % 4)]] if first else (),
                                    mark=last)
                            first = False
                    res_ic[oc] = vop(lambda e, o=oc, bk=bank: e.tensor_tensor(
                        xbuf[:, o, :], xbuf[:, o, :], bk[:, :], ALU.add),
                        wait=[em])
                    bfree[4 + (oc % 4)] = res_ic[oc]
                wb_free[wi] = em
            return res_ic

        load_chunk(0)
        if NCI > 1:
            load_chunk(1)
        prev_kv = None
        pa = ffn_pre_a(xb[0], x_load_ev[0])
        late_prologue_a1()
        e_q = ffn_pre_b(pa)
        e_f1 = ffn_mm(xb[0], e_q)
        late_prologue_a2()
        ln_ic = attn_ln_b(attn_ln_a(xb[0], e_f1))
        late_prologue_b()
        for ci in range(NCI):
            bi = x_buf_of[ci]
            xbuf = xb[bi]
            if taps:
                dma(SY, tp_x1[:, :, ci * TT:(ci + 1) * TT]
                    .rearrange("c p t -> p c t"), xbuf[:, :, :], wait=e_f1)
            kv_par = ci % 2
            e_kv = qkv(ci, kv_par, ln_ic)
            if ci > 0:
                e_at = attention(ci, kv_par, e_kv, prev_kv)
            prev_kv = e_kv
            if ci == 0:
                xb_free[bi] = e_kv
            pa_next = None
            if ci + 1 < NCI:
                pa_next = ffn_pre_a(xb[x_buf_of[ci + 1]], x_load_ev[ci + 1])
            if ci > 0:
                e_op = outproj(xbuf, e_at)
                if taps:
                    dma(SY, tp_x2[:, :, ci * TT:(ci + 1) * TT]
                        .rearrange("c p t -> p c t"), xbuf[:, :, :], wait=e_op)
            e_f1_next = None
            if pa_next is not None:
                e_qn = ffn_pre_b(pa_next)
                e_f1_next = ffn_mm(xb[x_buf_of[ci + 1]], e_qn)
                pa_ln = attn_ln_a(xb[x_buf_of[ci + 1]], e_f1_next)
            if ci > 0:
                pre2 = ffn_pre_b(ffn_pre_a(xbuf, e_op))
                e_f2 = ffn_mm(xbuf, pre2)
                es = dma(SY, xout[:, :, (ci - 1) * TT:ci * TT]
                         .rearrange("c p t -> p c t"), xbuf[:, :, :], wait=e_f2)
                xb_free[bi] = es
            if pa_next is not None:
                ln_ic = attn_ln_b(pa_ln)
                e_f1 = e_f1_next
            if ci + 2 < NCI:
                load_chunk(ci + 2)

        with (
            nc.semaphore("sm_t") as sm_t,
            nc.semaphore("sm_v") as sm_v,
            nc.semaphore("sm_a") as sm_a,
            nc.semaphore("sm_s") as sm_s,
            nc.semaphore("sm_g") as sm_g,
            nc.Block() as block,
        ):
            sems = {T: sm_t, V: sm_v, A: sm_a, SY: sm_s, G: sm_g}
            for n, s in b.st.items():
                s.sem = sems[n]

            def emit(engobj, s):
                hi = {}
                for ws, fn, inc in s.ops:
                    for (ps, v) in ws:
                        k = id(ps.sem)
                        if hi.get(k, -1) >= v:
                            continue
                        engobj.wait_ge(ps.sem, v)
                        hi[k] = v
                    ins = fn(engobj)
                    if inc and ins is not None:
                        ins.then_inc(s.sem, inc)

            @block.tensor
            def _(tensor):
                emit(tensor, b.st[T])

            @block.vector
            def _(vector):
                emit(vector, b.st[V])

            @block.scalar
            def _(scalar):
                emit(scalar, b.st[A])

            @block.gpsimd
            def _(gpsimd):
                emit(gpsimd, b.st[G])

            @block.sync
            def _(sync):
                emit(sync, b.st[SY])
                sync.wait_ge(sm_s, b.st[SY].cnt)

    return nc


def _prep_inputs(L0_attn_x, L0_attn_w_qkv, L0_attn_w_out, L0_ffn_ternary_weight,
                 L0_ffn_bias, L0_ffn_act_scale):
    x = np.ascontiguousarray(np.asarray(L0_attn_x, dtype=np.float32).reshape(S, D))
    wq = np.ascontiguousarray(np.asarray(L0_attn_w_qkv, dtype=np.float32))
    wo = np.ascontiguousarray(np.asarray(L0_attn_w_out, dtype=np.float32))
    tw = np.ascontiguousarray(np.asarray(L0_ffn_ternary_weight, dtype=np.int32))
    bs = np.ascontiguousarray(np.asarray(L0_ffn_bias, dtype=np.float32))
    sc = np.asarray(L0_ffn_act_scale, dtype=np.float32).reshape(1, 1)

    wq_r = wq.reshape(NIC, 128, QKVC)
    wo_r = wo.reshape(NIC, 128, D)
    tw_r = np.ascontiguousarray(tw.T).reshape(NIC, 128, D)
    bs_r = np.ascontiguousarray(bs.reshape(NIC, 128).T)

    tri = np.zeros((128, 128), dtype=np.float32)
    tri[np.arange(128)[:, None] > np.arange(128)[None, :]] = MASKV
    ones_r = np.ones((128, 1), dtype=np.float32)
    ones_row = np.ones((1, 128), dtype=np.float32)

    in_maps = []
    for c in range(NCORE):
        t0 = c * OWN * W
        xc = np.zeros((NCI * W, D), dtype=np.float32)
        if c == 0:
            xc[W:] = x[0:OWN * W]
        else:
            xc[:] = x[t0 - W:t0 + OWN * W]
        xin = np.ascontiguousarray(xc.T).reshape(NIC, 128, NCI * W)
        prevm = np.full((128, W), MASKV if c == 0 else 0.0, dtype=np.float32)
        in_maps.append({
            "xin": xin, "wqkv": wq_r, "wout": wo_r, "tern": tw_r,
            "bias8": bs_r, "scal": sc, "c_tri": tri, "c_prev": prevm,
            "c_ones_r": ones_r, "c_ones_row": ones_row,
        })
    return in_maps


def kernel(L0_attn_x, L0_attn_w_qkv, L0_attn_w_out, L0_ffn_ternary_weight,
           L0_ffn_bias, L0_ffn_act_scale):
    in_maps = _prep_inputs(L0_attn_x, L0_attn_w_qkv, L0_attn_w_out,
                           L0_ffn_ternary_weight, L0_ffn_bias, L0_ffn_act_scale)
    nc = build_nc(taps=False)
    res = run_bass_kernel_spmd(nc, in_maps, core_ids=list(range(NCORE)))
    out = np.empty((S, D), dtype=np.float32)
    for c in range(NCORE):
        xo = np.asarray(res.results[c]["xout"])
        out[c * OWN * W:(c + 1) * OWN * W] = xo.reshape(D, OWN * W).T
    return out.reshape(1, S, D)


# version 27 (final)
# speedup vs baseline: 1.0175x; 1.0175x over previous
import contextlib
import numpy as np
import ml_dtypes
import concourse.bass as bass
import concourse.mybir as mybir
from concourse.bass_utils import run_bass_kernel_spmd

F32 = mybir.dt.float32
F32R = mybir.dt.float32r
BF16 = mybir.dt.bfloat16
I32 = mybir.dt.int32
AF = mybir.ActivationFunctionType
ALU = mybir.AluOpType

S, D, H, HD, W = 16384, 1024, 16, 64, 512
NCORE = 8
C = S // W
OWN = C // NCORE
NCI = OWN + 1
NIC = D // 128
TT = W
NTT = W // 128
QKVC = 3 * D
MASKV = -30000.0
INV1024 = 1.0 / 1024.0


class _St:
    def __init__(self, name, dma):
        self.name, self.dma = name, dma
        self.ops = []
        self.cnt = 0
        self.hw = {}


class _Bld:
    def __init__(self):
        self.st = {n: _St(n, n in ("sync", "gpsimd")) for n in
                   ("tensor", "vector", "scalar", "sync", "gpsimd")}

    def op(self, eng, fn, wait=(), mark=False):
        s = self.st[eng]
        ws = []
        for w in wait:
            if w is None:
                continue
            ps, v = w
            if ps is s or v <= 0:
                continue
            if s.hw.get(ps.name, -1) >= v:
                continue
            s.hw[ps.name] = v
            ws.append((ps, v))
        inc = 16 if s.dma else (1 if mark else 0)
        s.ops.append((ws, fn, inc))
        if inc:
            s.cnt += inc
        return (s, s.cnt)

    def ev(self, eng):
        s = self.st[eng]
        return (s, s.cnt)


def build_nc(taps=False):
    nc = bass.Bass("TRN2", target_bir_lowering=False)
    b = _Bld()
    T, V, A, SY, G = "tensor", "vector", "scalar", "sync", "gpsimd"

    xin = nc.dram_tensor("xin", [NIC, 128, NCI * TT], F32, kind="ExternalInput").ap()
    wqkv = nc.dram_tensor("wqkv", [NIC, 128, QKVC], F32, kind="ExternalInput").ap()
    wout = nc.dram_tensor("wout", [NIC, 128, D], F32, kind="ExternalInput").ap()
    tern = nc.dram_tensor("tern", [NIC, 128, D], I32, kind="ExternalInput").ap()
    bias8 = nc.dram_tensor("bias8", [128, NIC], F32, kind="ExternalInput").ap()
    scal = nc.dram_tensor("scal", [1, 1], F32, kind="ExternalInput").ap()
    c_tri = nc.dram_tensor("c_tri", [128, 128], F32, kind="ExternalInput").ap()
    c_prev = nc.dram_tensor("c_prev", [128, TT], F32, kind="ExternalInput").ap()
    c_ones_r = nc.dram_tensor("c_ones_r", [128, 1], F32R, kind="ExternalInput").ap()
    c_ones_row = nc.dram_tensor("c_ones_row", [1, 128], F32, kind="ExternalInput").ap()
    xout = nc.dram_tensor("xout", [NIC, 128, OWN * TT], F32, kind="ExternalOutput").ap()

    s_wqh = nc.dram_tensor("s_wqh", [NIC, 128, QKVC], BF16).ap()
    s_wql = nc.dram_tensor("s_wql", [NIC, 128, QKVC], BF16).ap()
    s_woh = nc.dram_tensor("s_woh", [NIC, 128, D], BF16).ap()
    s_wol = nc.dram_tensor("s_wol", [NIC, 128, D], BF16).ap()
    s_tern = nc.dram_tensor("s_tern", [NIC, 128, D], BF16).ap()

    if taps:
        tp_rbc = nc.dram_tensor("tp_rbc", [128, 512], F32, kind="ExternalOutput").ap()
        tp_s0 = nc.dram_tensor("tp_s0", [128, 512], F32, kind="ExternalOutput").ap()
        tp_row = nc.dram_tensor("tp_row", [1, 512], F32, kind="ExternalOutput").ap()
        tp_x1 = nc.dram_tensor("tp_x1", [NIC, 128, NCI * TT], F32, kind="ExternalOutput").ap()
        tp_x2 = nc.dram_tensor("tp_x2", [NIC, 128, NCI * TT], F32, kind="ExternalOutput").ap()

    ctx = contextlib.ExitStack()
    sbt = lambda name, shape, dt: ctx.enter_context(nc.sbuf_tensor(name, shape, dt))
    psb = lambda name: ctx.enter_context(nc.psum_tensor(name, [128, 512], F32))

    with ctx:
        xb = [sbt(f"xb{i}", [128, NIC, TT], F32) for i in range(2)]
        xq = sbt("xq", [128, NIC, TT], BF16)
        s0 = sbt("s0", [128, 512], F32)
        s1 = sbt("s1", [128, 512], F32)
        i0 = sbt("i0", [128, 512], I32)
        i1 = sbt("i1", [128, 512], I32)
        g0 = sbt("g0", [128, 512], F32)
        g1 = sbt("g1", [128, 512], F32)
        srr = sbt("srr", [128, 512], F32R)
        sqr = sbt("sqr", [128, 512], F32R)
        ahi = sbt("ahi", [128, NIC, TT], BF16)
        alo = sbt("alo", [128, NIC, TT], BF16)
        qh = sbt("qh", [128, NIC, TT], BF16)
        ql = sbt("ql", [128, NIC, TT], BF16)
        kh = [sbt(f"kh{i}", [128, NIC, TT], BF16) for i in range(2)]
        kl = [sbt(f"kl{i}", [128, NIC, TT], BF16) for i in range(2)]
        vt = [sbt(f"vt{i}", [128, NTT, H * 65], F32) for i in range(2)]
        wbh = [sbt(f"wbh{i}", [128, NIC, 512], BF16) for i in range(2)]
        wbl = [sbt(f"wbl{i}", [128, NIC, 512], BF16) for i in range(2)]
        rows = sbt("rows", [1, 8 * 512], F32)
        rbc = sbt("rbc", [128, 512], F32)
        tri = sbt("tri", [128, 128], F32)
        prevm = sbt("prevm", [128, TT], F32)
        onesr = sbt("onesr", [128, 1], F32R)
        onesrow = sbt("onesrow", [1, 128], F32)
        hb = sbt("hb", [128, NIC], F32)
        hsbc = sbt("hsbc", [128, 1], F32)
        sc_r = sbt("sc_r", [1, 4], F32)

        PS = [psb(f"ps{i}") for i in range(8)]

        R_MEAN = rows[0:1, 0 * 512:1 * 512]
        R_MS = rows[0:1, 1 * 512:2 * 512]
        R_U = rows[0:1, 2 * 512:3 * 512]
        R_T1 = rows[0:1, 3 * 512:4 * 512]
        R_T2 = rows[0:1, 4 * 512:5 * 512]
        R_LA = rows[0:1, 5 * 512:6 * 512]
        R_LB = rows[0:1, 6 * 512:7 * 512]

        Pt = [[g0[:, :], g1[:, :]],
              [i0[:, :].bitcast(F32), i1[:, :].bitcast(F32)]]
        Pr = [srr, sqr]

        bfree = {i: None for i in range(8)}
        rows_tread = [None]
        s_free = [None, None]
        sq_scr_free = [None, None]

        def dma(eng, out_ap, in_ap, wait=()):
            return b.op(eng, lambda e, o=out_ap, i=in_ap: e.dma_start(out=o, in_=i),
                        wait=wait)

        def mm(out_ap, lh, rh, start, stop, wait=(), mark=False, tp=None):
            def f(e, o=out_ap, l=lh, r=rh, st=start, sp=stop, t=tp):
                return e.matmul(o, l, r, start=st, stop=sp, tile_position=t)
            return b.op(T, f, wait=wait, mark=mark)

        def vop(fn, wait=()):
            return b.op(V, fn, wait=wait, mark=True)

        def aop(fn, wait=()):
            return b.op(A, fn, wait=wait, mark=True)

        dma(SY, tri[:, :], c_tri)
        dma(SY, prevm[:, :], c_prev)
        dma(SY, onesr[:, :], c_ones_r)
        dma(SY, onesrow[:, :], c_ones_row)
        dma(SY, hb[:, :], bias8)
        e_cst = dma(SY, sc_r[0:1, 0:1], scal)
        vop(lambda e: e.tensor_scalar_mul(hb[:, :], hb[:, :], 0.5), wait=[e_cst])
        e_sm = vop(lambda e: e.tensor_scalar_max(sc_r[0:1, 0:1], sc_r[0:1, 0:1], 1e-5),
                   wait=[e_cst])
        vop(lambda e: e.reciprocal(sc_r[0:1, 1:2], sc_r[0:1, 0:1]), wait=[e_sm])
        e_hs = vop(lambda e: e.tensor_scalar_mul(sc_r[0:1, 2:3], sc_r[0:1, 0:1], 0.5),
                   wait=[e_sm])
        e_m = mm(PS[0][:, 0:1], onesrow[:, :], sc_r[0:1, 2:3], True, True,
                 wait=[e_hs, e_cst], mark=True)
        vop(lambda e: e.tensor_copy(hsbc[:, :], PS[0][:, 0:1]), wait=[e_m])

        e_prev = None
        for sl in range(2):
            colsl = slice(sl * 512, (sl + 1) * 512)
            e1 = dma(G, kh[0][:, :, :],
                     tern[:, :, colsl].rearrange("c p o -> p c o"),
                     wait=[e_prev] if e_prev else ())
            e2 = vop(lambda e: e.tensor_scalar_sub(kl[0][:, :, :],
                                                   kh[0][:, :, :], 1.0), wait=[e1])
            e_prev = dma(G, s_tern[:, :, colsl].rearrange("c p o -> p c o"),
                         kl[0][:, :, :], wait=[e2])
        e_tern_ready = e_prev
        pro_free = [None, None, e_tern_ready, None, None]

        pro_scr = [vvt[:, :, :].rearrange("p a b -> p (a b)")[:, 0:4096]
                   .rearrange("p (c t) -> p c t", c=NIC) for vvt in vt]

        e_wq_sl = [None] * 6
        e_wo_sl = [None] * 2
        xb_free = [None, None]
        wb_free = [None, None]
        pro_i = [0]

        def split_slices(tasks, sps=None):
            stg = [(kh[0], kl[0]), (kh[1], kl[1]), (qh, ql)]
            last = None
            loads = []
            i0 = pro_i[0]
            for ti, (src, dsth, dstl, sl, rl) in enumerate(tasks):
                i = i0 + ti
                par = i % 2
                colsl = slice(sl * 512, (sl + 1) * 512)
                loads.append(dma(SY, pro_scr[par],
                                 src[:, :, colsl].rearrange("c p t -> p c t"),
                                 wait=[pro_free[par]]))
            for ti, (src, dsth, dstl, sl, rl) in enumerate(tasks):
                i = pro_i[0]
                pro_i[0] += 1
                par = i % 2
                sp = (i % 3) if sps is None else sps[ti]
                colsl = slice(sl * 512, (sl + 1) * 512)
                scr = pro_scr[par]
                wh, wl = stg[sp]
                eh = aop(lambda e, s_=scr, w=wh: e.copy(w[:, :, :], s_),
                         wait=[loads[ti], pro_free[2 + sp]])
                el = vop(lambda e, s_=scr, w=wl, h=wh: e.tensor_tensor(
                    w[:, :, :], s_, h[:, :, :], ALU.subtract), wait=[eh])
                pro_free[par] = el
                e2 = dma(SY, dsth[:, :, colsl].rearrange("c p o -> p c o"),
                         wh[:, :, :], wait=[eh])
                e3 = dma(SY, dstl[:, :, colsl].rearrange("c p o -> p c o"),
                         wl[:, :, :], wait=[el])
                pro_free[2 + sp] = e3
                rl[sl] = e3
                last = e3
            return last

        def late_prologue_a1():
            return split_slices([(wqkv, s_wqh, s_wql, sl, e_wq_sl)
                                 for sl in (2, 3)])

        def late_prologue_a2():
            return split_slices([(wqkv, s_wqh, s_wql, sl, e_wq_sl)
                                 for sl in (4, 5)])

        def late_prologue_b():
            split_slices([(wqkv, s_wqh, s_wql, sl, e_wq_sl)
                          for sl in (0, 1)], sps=(1, 2))
            e_wo_ready = split_slices([(wout, s_woh, s_wol, sl, e_wo_sl)
                                       for sl in (0, 1)], sps=(1, 2))
            for i in range(2):
                vop(lambda e, vv=vt[i]: e.memset(
                    vv[:, :, :].rearrange("p a (h d) -> p a h d", h=H)
                    [:, :, :, 64:65], 1.0))
            return e_wo_ready
        evac_free = [None, None]

        x_load_ev, x_buf_of = [], []

        def load_chunk(ci):
            bi = ci % 2
            e = dma(SY, xb[bi][:, :, :],
                    xin[:, :, ci * TT:(ci + 1) * TT].rearrange("c p t -> p c t"),
                    wait=[xb_free[bi]])
            x_load_ev.append(e)
            x_buf_of.append(bi)
            return e

        def ln_stats(xbuf, x_ready, inv_seg=None):
            if not isinstance(x_ready, list):
                x_ready = [x_ready] * NIC
            """mean -> R_MEAN, invstd -> R_T1 (rows on partition 0).
            sum(x) via direct fp32 matmuls on xbuf; sum(x^2) via ACT Square
            into s0/s1 (parity) then fp32 matmuls - full fp32 statistics."""
            e_sum = e_sq = None
            onesf = onesr[:, :].bitcast(F32)
            for ic in range(NIC):
                sq = srr if ic % 2 == 0 else sqr
                e_sum = mm(PS[2][0:1, :], onesf, xbuf[:, ic, :],
                           ic == 0, ic == NIC - 1,
                           wait=[x_ready[ic], bfree[2]] if ic == 0
                           else [x_ready[ic]], mark=True)
                eq = aop(lambda e, i=ic, s=sq: e.activation(
                    s[:, :], xbuf[:, i, :], AF.Square),
                    wait=[x_ready[ic], sq_scr_free[ic % 2]])
                e_sq = mm(PS[3][0:1, :], onesr[:, :], sq[:, :],
                          ic == 0, ic == NIC - 1,
                          wait=[eq, bfree[3]] if ic == 0 else [eq], mark=True)
                sq_scr_free[ic % 2] = e_sq
            e1 = vop(lambda e: e.tensor_scalar_mul(R_MEAN, PS[2][0:1, :], INV1024),
                     wait=[e_sum, rows_tread[0], attention.rows_dma])
            bfree[2] = e1
            e2 = vop(lambda e: e.tensor_scalar_mul(R_MS, PS[3][0:1, :], INV1024),
                     wait=[e_sq])
            bfree[3] = e2
            e3 = vop(lambda e: e.tensor_mul(R_T1, R_MEAN, R_MEAN), wait=[e1])
            e5 = vop(lambda e: e.scalar_tensor_tensor(R_U, R_MS, 1e-5, R_T1,
                                                      ALU.add, ALU.subtract),
                     wait=[e2, e3])
            e6 = aop(lambda e: e.activation(R_T1, R_U, AF.Sqrt), wait=[e5])
            e7 = vop(lambda e: e.reciprocal(R_T2, R_T1), wait=[e6])
            e8 = vop(lambda e: e.tensor_mul(R_T1, R_T2, R_T2), wait=[e7])
            e9 = vop(lambda e: e.tensor_mul(R_T1, R_U, R_T1), wait=[e8])
            ea = vop(lambda e: e.tensor_scalar(R_T1, R_T1, -0.5, 1.5,
                                               ALU.mult, ALU.add), wait=[e9])
            out = inv_seg if inv_seg is not None else R_T1
            return vop(lambda e, o=out: e.tensor_mul(o, R_T2, R_T1), wait=[ea])

        def bcast_rows(arow, brow, wait):
            eA = mm(PS[0][:, :], onesrow[:, :], arow, True, True,
                    wait=list(wait) + [bfree[0]], mark=True)
            eB = mm(PS[1][:, :], onesrow[:, :], brow, True, True,
                    wait=list(wait) + [bfree[1]], mark=True)
            rows_tread[0] = eB
            return eA, eB

        def ffn_pre_a(xbuf, x_ready):
            if not isinstance(x_ready, list):
                x_ready = [x_ready] * NIC
            e_inv = ln_stats(xbuf, x_ready)
            e_al = vop(lambda e: e.tensor_scalar_mul(R_T2, R_T1, sc_r[0:1, 1:2]),
                       wait=[e_inv])
            e_b1 = vop(lambda e: e.scalar_tensor_tensor(
                R_MS, R_MEAN, -1.0, R_T2, ALU.mult, ALU.mult), wait=[e_al])
            e_b2 = vop(lambda e: e.tensor_scalar_add(R_MS, R_MS, 0.5), wait=[e_b1])
            return (xbuf, x_ready, e_b2)

        def ffn_pre_b(parta):
            xbuf, x_ready, e_b2 = parta
            eA, eB = bcast_rows(R_T2, R_MS, [e_b2])
            e_q = None

            def q_head(ic):
                sA, iA = (s0, i0) if ic % 2 == 0 else (s1, i1)
                e1 = vop(lambda e, i=ic, s=sA: e.tensor_tensor(
                    s[:, :], xbuf[:, i, :], PS[0][:, :], ALU.mult),
                    wait=[eA, x_ready[ic]])
                bfree[0] = e1
                e2 = vop(lambda e, s=sA: e.tensor_tensor(
                    s[:, :], s[:, :], PS[1][:, :], ALU.add), wait=[eB, e1])
                bfree[1] = e2
                return aop(lambda e, s=sA, ii=iA: e.copy(ii[:, :], s[:, :]),
                           wait=[e2])

            def q_tail(ic, e3):
                sA, iA, gA = (s0, i0, g0) if ic % 2 == 0 else (s1, i1, g1)
                e4 = vop(lambda e, s=sA, ii=iA, g=gA: e.tensor_tensor(
                    g[:, :], ii[:, :], s[:, :], ALU.is_gt), wait=[e3])
                e5 = vop(lambda e, s=sA, ii=iA, g=gA: e.tensor_tensor(
                    s[:, :], ii[:, :], g[:, :], ALU.subtract), wait=[e4])
                e6 = vop(lambda e, i=ic, s=sA: e.tensor_scalar(
                    xq[:, i, :], s[:, :], 127.0, -127.0, ALU.min, ALU.max),
                    wait=[e5])
                s_free[ic % 2] = e6
                return e6

            pend = []
            e_q_ic = [None] * NIC
            for ic in range(NIC):
                pend.append((ic, q_head(ic)))
                if len(pend) == 2:
                    ic0, e3 = pend.pop(0)
                    e_q_ic[ic0] = q_tail(ic0, e3)
            while pend:
                ic0, e3 = pend.pop(0)
                e_q_ic[ic0] = q_tail(ic0, e3)
            return e_q_ic

        def ffn_mm(xbuf, e_q_ic):
            res_ic = [None] * NIC
            e_res = None
            for sl in range(2):
                wi = sl % 2
                e_w = dma(G, wbh[wi][:, :, :],
                          s_tern[:, :, sl * 512:(sl + 1) * 512]
                          .rearrange("c p o -> p c o"),
                          wait=[e_tern_ready, wb_free[wi]])
                em = None
                for j0 in (0, 2):
                    ocs = (sl * 4 + j0, sl * 4 + j0 + 1)
                    bks = [PS[4 + (o % 4)] for o in ocs]
                    ems = [None, None]
                    for ic in range(NIC):
                        for t in (0, 1):
                            ems[t] = mm(
                                bks[t][:, :],
                                wbh[wi][:, ic,
                                        (j0 + t) * 128:(j0 + t + 1) * 128],
                                xq[:, ic, :], ic == 0, ic == NIC - 1,
                                wait=[e_w, bfree[4 + (ocs[t] % 4)],
                                      e_q_ic[ic]] if ic == 0
                                else [e_q_ic[ic]], mark=(ic == NIC - 1))
                    for t in (0, 1):
                        oc = ocs[t]
                        bank = bks[t]
                        gg = g0 if oc % 2 == 0 else g1
                        et = aop(lambda e, bk=bank, o=oc, g=gg: e.activation(
                            g[:, :], bk[:, :], AF.Identity,
                            bias=hb[:, o:o + 1], scale=hsbc[:, 0:1]),
                            wait=[ems[t], evac_free[oc % 2]])
                        bfree[4 + (oc % 4)] = et
                        e_res = vop(lambda e, o=oc, g=gg: e.tensor_tensor(
                            xbuf[:, o, :], xbuf[:, o, :], g[:, :], ALU.add),
                            wait=[et])
                        evac_free[oc % 2] = e_res
                        res_ic[oc] = e_res
                    em = ems[1]
                wb_free[wi] = em
            return res_ic

        def attn_ln_a(xbuf, x_ready):
            if not isinstance(x_ready, list):
                x_ready = [x_ready] * NIC
            e_inv = ln_stats(xbuf, x_ready, inv_seg=R_LA)
            e_b1 = vop(lambda e: e.scalar_tensor_tensor(
                R_LB, R_MEAN, -1.0, R_LA, ALU.mult, ALU.mult), wait=[e_inv])
            return (xbuf, x_ready, e_b1)

        def attn_ln_b(parta):
            xbuf, x_ready, e_b1 = parta
            eA, eB = bcast_rows(R_LA, R_LB, [e_b1])
            e_last = None
            ln_ic = [None] * NIC
            for ic in range(NIC):
                sA = s0 if ic % 2 == 0 else s1
                e1 = vop(lambda e, i=ic, s=sA: e.tensor_tensor(
                    s[:, :], xbuf[:, i, :], PS[0][:, :], ALU.mult),
                    wait=[eA, x_ready[ic]])
                bfree[0] = e1
                e2 = vop(lambda e, s=sA: e.tensor_tensor(
                    s[:, :], s[:, :], PS[1][:, :], ALU.add), wait=[eB, e1])
                bfree[1] = e2
                e3 = vop(lambda e, i=ic, s=sA: e.tensor_copy(ahi[:, i, :], s[:, :]),
                         wait=[e2])
                ln_ic[ic] = vop(lambda e, i=ic, s=sA: e.tensor_tensor(
                    alo[:, i, :], s[:, :], ahi[:, i, :], ALU.subtract), wait=[e3])
                s_free[ic % 2] = ln_ic[ic]
            return ln_ic

        def qkv(ci, kv_par, ln_ic):
            halo = (ci == 0)
            e_done = None
            for sl in range(6):
                wi = sl % 2
                cols = slice(sl * 512, (sl + 1) * 512)
                is_q = sl < 2
                is_k = 2 <= sl < 4
                if is_q and halo:
                    continue
                e_wh = dma(G, wbh[wi][:, :, :],
                           s_wqh[:, :, cols].rearrange("c p o -> p c o"),
                           wait=[e_wq_sl[sl], wb_free[wi]])
                e_wl = dma(G, wbl[wi][:, :, :],
                           s_wql[:, :, cols].rearrange("c p o -> p c o"),
                           wait=[e_wq_sl[sl], wb_free[wi]])
                em_last = None
                if is_q or is_k:
                    dsth = qh if is_q else kh[kv_par]
                    dstl = ql if is_q else kl[kv_par]
                    base = (sl - (0 if is_q else 2)) * 4
                    for j0 in (0, 2):
                        fcs = (base + j0, base + j0 + 1)
                        bks = [PS[4 + (f % 4)] for f in fcs]
                        ems = [None, None]
                        firsts = [True, True]
                        for ic in range(NIC):
                            for (wt, at_) in ((wbh[wi], ahi), (wbh[wi], alo),
                                              (wbl[wi], ahi)):
                                last = (ic == NIC - 1) and (wt is wbl[wi])
                                for t in (0, 1):
                                    ems[t] = mm(
                                        bks[t][:, :],
                                        wt[:, ic, (j0 + t) * 128:
                                           (j0 + t + 1) * 128],
                                        at_[:, ic, :], firsts[t], last,
                                        wait=[e_wh, e_wl, ln_ic[ic],
                                              bfree[4 + (fcs[t] % 4)]]
                                        if firsts[t] else [ln_ic[ic]],
                                        mark=last)
                                    firsts[t] = False
                        for t in (0, 1):
                            fc = fcs[t]
                            bank = bks[t]
                            sfence = []
                            if fc == base:
                                if ci == 0 and is_k:
                                    sfence = [pro_free[2]]
                                elif ci == 1:
                                    sfence = ([pro_free[3]] if is_k
                                              else [pro_free[4]])
                            eh = vop(lambda e, f=fc, bk=bank, d=dsth:
                                     e.tensor_copy(d[:, f, :], bk[:, :]),
                                     wait=[ems[t]] + sfence)
                            e_done = vop(lambda e, f=fc, bk=bank, dh=dsth,
                                         dl=dstl:
                                         e.tensor_tensor(dl[:, f, :], bk[:, :],
                                                         dh[:, f, :],
                                                         ALU.subtract),
                                         wait=[eh])
                            bfree[4 + (fc % 4)] = e_done
                        em_last = ems[1]
                else:
                    vcol = slice((sl - 4) * 512, (sl - 3) * 512)
                    for t0 in range(0, NTT, 2):
                        tts = (t0, t0 + 1)
                        bks = [PS[4 + (t % 4)] for t in tts]
                        ems = [None, None]
                        firsts = [True, True]
                        for ic in range(NIC):
                            for (at_, wt) in ((ahi, wbh[wi]), (alo, wbh[wi]),
                                              (ahi, wbl[wi])):
                                last = (ic == NIC - 1) and (wt is wbl[wi])
                                for t in (0, 1):
                                    ems[t] = mm(
                                        bks[t][:, :],
                                        at_[:, ic, tts[t] * 128:
                                            (tts[t] + 1) * 128],
                                        wt[:, ic, :], firsts[t], last,
                                        wait=[e_wh, e_wl, ln_ic[ic],
                                              bfree[4 + (tts[t] % 4)]]
                                        if firsts[t] else [ln_ic[ic]],
                                        mark=last)
                                    firsts[t] = False
                        h0 = (sl - 4) * 8
                        for t in (0, 1):
                            tt = tts[t]
                            bank = bks[t]
                            e_done = vop(lambda e, t_=tt, bk=bank, hh=h0:
                                         e.tensor_copy(
                                             vt[kv_par][:, t_, :]
                                             .rearrange("p (h d) -> p h d",
                                                        h=H)
                                             [:, hh:hh + 8, 0:64],
                                             bk[:, :].rearrange(
                                                 "p (h d) -> p h d", h=8)),
                                         wait=[ems[t]])
                            bfree[4 + (tt % 4)] = e_done
                        em_last = ems[1]
                wb_free[wi] = em_last
            return e_done

        def attention(ci, kv_par, e_kv_cur, e_kv_prev):
            first_local = (ci == 1)
            e_out = None
            pv_ev = attention.pv_ev
            i1f = i1[:, :].bitcast(F32)
            pq = []
            for p in range(8):
                em_O = [None, None]
                for j in range(8):
                    khb = kh[1 - kv_par] if j < 4 else kh[kv_par]
                    klb = kl[1 - kv_par] if j < 4 else kl[kv_par]
                    vbuf = vt[1 - kv_par] if j < 4 else vt[kv_par]
                    jj = j % 4
                    qlo = 128 * (j - 4) if j > 4 else 0
                    par = j % 2
                    banks = (PS[4 + par], PS[6 + par])
                    em_s = [None, None]
                    for ih in (0, 1):
                        bank = banks[ih]
                        pr = slice(64 * ih, 64 * ih + 64)
                        first = True
                        for (kk, qq) in ((khb, qh), (khb, ql), (klb, qh)):
                            em_s[ih] = mm(bank[:, qlo:512],
                                          kk[pr, p, jj * 128:(jj + 1) * 128],
                                          qq[pr, p, qlo:512], first, kk is klb,
                                          wait=[e_kv_cur,
                                                e_kv_prev if j < 4 else None,
                                                bfree[4 + 2 * ih + par]]
                                          if first else (),
                                          mark=(kk is klb), tp=(64 * ih, 0))
                            first = False
                    mevs = [[em_s[0]], [em_s[1]]]
                    for ih in (0, 1):
                        bank = banks[ih]
                        if j < 4 and first_local:
                            mevs[ih].append(vop(
                                lambda e, bk=bank: e.tensor_tensor(
                                    bk[:, :], bk[:, :], prevm[:, :], ALU.add),
                                wait=[em_s[ih]]))
                        if j >= 4:
                            qs = slice(128 * (j - 4), 128 * (j - 4) + 128)
                            mevs[ih].append(vop(
                                lambda e, bk=bank, q=qs: e.tensor_tensor(
                                    bk[:, q], bk[:, q], tri[:, :], ALU.add),
                                wait=[em_s[ih]]))
                    if j == 4:
                        finish_pair()
                    if len(pq) == 2:
                        emit_tail(pq.pop(0))
                    ee = [None, None]
                    for ih in (0, 1):
                        P = Pt[ih][par]
                        xw = [attention.i1_free] if (ih == 1 and par == 1) else []
                        ee[ih] = aop(lambda e, bk=banks[ih], P_=P, q0=qlo:
                                     e.activation(P_[:, q0:512], bk[:, q0:512],
                                                  AF.Exp, scale=0.125),
                                     wait=mevs[ih] + [pv_ev[ih][par]] + xw)
                        bfree[4 + 2 * ih + par] = ee[ih]
                    pq.append((p, em_O, j, ee, qlo, jj, vbuf))
                while pq:
                    emit_tail(pq.pop(0))
                e_oA = vop(lambda e: e.tensor_copy(s0[0:64, :], PS[3][0:64, :]),
                           wait=[em_O[0], s_free[0]])
                e_sA = vop(lambda e: e.tensor_copy(s1[64:65, :], PS[3][64:65, :]),
                           wait=[em_O[0], attention.s1_free])
                bfree[3] = e_sA
                e_oB = vop(lambda e: e.tensor_copy(s1[0:64, :], PS[1][0:64, :]),
                           wait=[em_O[1], attention.s1_free])
                e_sB = vop(lambda e: e.tensor_copy(i1f[64:65, :],
                                                   PS[1][64:65, :]),
                           wait=[em_O[1], attention.i1_free])
                bfree[1] = e_sB
                d_oB = dma(SY, s0[64:128, :], s1[0:64, :],
                           wait=[e_oB, s_free[0]])
                attention.s1_free = d_oB
                d_sA = dma(SY, R_MEAN, s1[64:65, :],
                           wait=[e_sA, rows_tread[0], attention.rows_dma])
                attention.s1_free = d_sA
                d_sB = dma(SY, R_MS, i1f[64:65, :],
                           wait=[e_sB, rows_tread[0]])
                attention.rows_dma = d_sB
                attention.i1_free = d_sB
                attention.pending = (p, d_sA, d_sB, d_oB, e_oA)
            finish_pair()
            return attention.last_out

        def finish_pair():
            if attention.pending is None:
                return
            p, d_sA, d_sB, d_oB, e_oA = attention.pending
            attention.pending = None
            e_bcA = mm(PS[0][0:64, :], onesrow[0:1, 0:64], R_MEAN, True, True,
                       wait=[d_sA, bfree[0]], mark=True)
            e_bc = mm(PS[0][64:128, :], onesrow[0:1, 0:64], R_MS, True, True,
                      wait=[d_sB], mark=True)
            rows_tread[0] = e_bc
            e_cp = vop(lambda e: e.reciprocal(rbc[:, :], PS[0][:, :]),
                       wait=[e_bcA, e_bc])
            bfree[0] = e_cp
            e_n = vop(lambda e: e.tensor_tensor(s0[:, :], s0[:, :],
                                                rbc[:, :], ALU.mult),
                      wait=[e_cp, e_oA, d_oB])
            e_h = vop(lambda e, pp=p: e.tensor_copy(ahi[:, pp, :], s0[:, :]),
                      wait=[e_n])
            e_out = vop(lambda e, pp=p: e.tensor_tensor(
                alo[:, pp, :], s0[:, :], ahi[:, pp, :], ALU.subtract),
                wait=[e_h])
            s_free[0] = e_out
            attention.last_out = e_out
            if taps and attention.dbg:
                attention.dbg = False
                dma(SY, tp_rbc, rbc[:, :], wait=[e_out])
                dma(SY, tp_s0, s0[:, :], wait=[e_out])
                dma(SY, tp_row, R_MEAN, wait=[e_out])


        attention.pv_ev = [[None, None], [None, None]]
        attention.s1_free = None
        attention.rows_dma = None
        attention.i1_free = None
        attention.pending = None
        attention.last_out = None
        attention.dbg = False

        def emit_tail(pend):
            p, em_O, j, ee, qlo, jj, vbuf = pend
            par = j % 2
            for ih in (0, 1):
                head = 2 * p + ih
                obank = PS[3] if ih == 0 else PS[1]
                em_O[ih] = mm(obank[0:65, qlo:512],
                              vbuf[:, jj, head * 65:head * 65 + 65],
                              Pt[ih][par][:, qlo:512], j == 0, j == 7,
                              wait=[ee[ih], bfree[3 if ih == 0 else 1]]
                              if j == 0 else [ee[ih]],
                              mark=True)
                attention.pv_ev[ih][par] = em_O[ih]

        def outproj(xbuf, e_at):
            res_ic = [None] * NIC
            for sl in range(2):
                wi = sl % 2
                cols = slice(sl * 512, (sl + 1) * 512)
                e_wh = dma(G, wbh[wi][:, :, :],
                           s_woh[:, :, cols].rearrange("c p o -> p c o"),
                           wait=[e_wo_sl[sl], wb_free[wi]])
                e_wl = dma(G, wbl[wi][:, :, :],
                           s_wol[:, :, cols].rearrange("c p o -> p c o"),
                           wait=[e_wo_sl[sl], wb_free[wi]])
                em = None
                for j0 in (0, 2):
                    ocs = (sl * 4 + j0, sl * 4 + j0 + 1)
                    bks = [PS[4 + (o % 4)] for o in ocs]
                    ems = [None, None]
                    firsts = [True, True]
                    for ic in range(NIC):
                        for (wt, at_) in ((wbh[wi], ahi), (wbh[wi], alo),
                                          (wbl[wi], ahi)):
                            last = (ic == NIC - 1) and (wt is wbl[wi])
                            for t in (0, 1):
                                ems[t] = mm(
                                    bks[t][:, :],
                                    wt[:, ic, (j0 + t) * 128:
                                       (j0 + t + 1) * 128],
                                    at_[:, ic, :], firsts[t], last,
                                    wait=[e_wh, e_wl, e_at,
                                          bfree[4 + (ocs[t] % 4)]]
                                    if firsts[t] else (), mark=last)
                                firsts[t] = False
                    for t in (0, 1):
                        oc = ocs[t]
                        res_ic[oc] = vop(lambda e, o=oc, bk=bks[t]:
                                         e.tensor_tensor(
                                             xbuf[:, o, :], xbuf[:, o, :],
                                             bk[:, :], ALU.add),
                                         wait=[ems[t]])
                        bfree[4 + (oc % 4)] = res_ic[oc]
                    em = ems[1]
                wb_free[wi] = em
            return res_ic

        load_chunk(0)
        if NCI > 1:
            load_chunk(1)
        prev_kv = None
        pa = ffn_pre_a(xb[0], x_load_ev[0])
        late_prologue_a1()
        e_q = ffn_pre_b(pa)
        e_f1 = ffn_mm(xb[0], e_q)
        late_prologue_a2()
        ln_ic = attn_ln_b(attn_ln_a(xb[0], e_f1))
        late_prologue_b()
        for ci in range(NCI):
            bi = x_buf_of[ci]
            xbuf = xb[bi]
            if taps:
                dma(SY, tp_x1[:, :, ci * TT:(ci + 1) * TT]
                    .rearrange("c p t -> p c t"), xbuf[:, :, :], wait=e_f1)
            kv_par = ci % 2
            e_kv = qkv(ci, kv_par, ln_ic)
            if ci > 0:
                e_at = attention(ci, kv_par, e_kv, prev_kv)
            prev_kv = e_kv
            if ci == 0:
                xb_free[bi] = e_kv
            pa_next = None
            if ci + 1 < NCI:
                pa_next = ffn_pre_a(xb[x_buf_of[ci + 1]], x_load_ev[ci + 1])
            if ci > 0:
                e_op = outproj(xbuf, e_at)
                if taps:
                    dma(SY, tp_x2[:, :, ci * TT:(ci + 1) * TT]
                        .rearrange("c p t -> p c t"), xbuf[:, :, :], wait=e_op)
            e_f1_next = None
            if pa_next is not None:
                e_qn = ffn_pre_b(pa_next)
                e_f1_next = ffn_mm(xb[x_buf_of[ci + 1]], e_qn)
                pa_ln = attn_ln_a(xb[x_buf_of[ci + 1]], e_f1_next)
            if ci > 0:
                pre2 = ffn_pre_b(ffn_pre_a(xbuf, e_op))
                e_f2 = ffn_mm(xbuf, pre2)
                es = dma(SY, xout[:, :, (ci - 1) * TT:ci * TT]
                         .rearrange("c p t -> p c t"), xbuf[:, :, :], wait=e_f2)
                xb_free[bi] = es
            if pa_next is not None:
                ln_ic = attn_ln_b(pa_ln)
                e_f1 = e_f1_next
            if ci + 2 < NCI:
                load_chunk(ci + 2)

        with (
            nc.semaphore("sm_t") as sm_t,
            nc.semaphore("sm_v") as sm_v,
            nc.semaphore("sm_a") as sm_a,
            nc.semaphore("sm_s") as sm_s,
            nc.semaphore("sm_g") as sm_g,
            nc.Block() as block,
        ):
            sems = {T: sm_t, V: sm_v, A: sm_a, SY: sm_s, G: sm_g}
            for n, s in b.st.items():
                s.sem = sems[n]

            def emit(engobj, s):
                hi = {}
                for ws, fn, inc in s.ops:
                    for (ps, v) in ws:
                        k = id(ps.sem)
                        if hi.get(k, -1) >= v:
                            continue
                        engobj.wait_ge(ps.sem, v)
                        hi[k] = v
                    ins = fn(engobj)
                    if inc and ins is not None:
                        ins.then_inc(s.sem, inc)

            @block.tensor
            def _(tensor):
                emit(tensor, b.st[T])

            @block.vector
            def _(vector):
                emit(vector, b.st[V])

            @block.scalar
            def _(scalar):
                emit(scalar, b.st[A])

            @block.gpsimd
            def _(gpsimd):
                emit(gpsimd, b.st[G])

            @block.sync
            def _(sync):
                emit(sync, b.st[SY])
                sync.wait_ge(sm_s, b.st[SY].cnt)

    return nc


def _prep_inputs(L0_attn_x, L0_attn_w_qkv, L0_attn_w_out, L0_ffn_ternary_weight,
                 L0_ffn_bias, L0_ffn_act_scale):
    x = np.ascontiguousarray(np.asarray(L0_attn_x, dtype=np.float32).reshape(S, D))
    wq = np.ascontiguousarray(np.asarray(L0_attn_w_qkv, dtype=np.float32))
    wo = np.ascontiguousarray(np.asarray(L0_attn_w_out, dtype=np.float32))
    tw = np.ascontiguousarray(np.asarray(L0_ffn_ternary_weight, dtype=np.int32))
    bs = np.ascontiguousarray(np.asarray(L0_ffn_bias, dtype=np.float32))
    sc = np.asarray(L0_ffn_act_scale, dtype=np.float32).reshape(1, 1)

    wq_r = wq.reshape(NIC, 128, QKVC)
    wo_r = wo.reshape(NIC, 128, D)
    tw_r = np.ascontiguousarray(tw.T).reshape(NIC, 128, D)
    bs_r = np.ascontiguousarray(bs.reshape(NIC, 128).T)

    tri = np.zeros((128, 128), dtype=np.float32)
    tri[np.arange(128)[:, None] > np.arange(128)[None, :]] = MASKV
    ones_r = np.ones((128, 1), dtype=np.float32)
    ones_row = np.ones((1, 128), dtype=np.float32)

    in_maps = []
    for c in range(NCORE):
        t0 = c * OWN * W
        xc = np.zeros((NCI * W, D), dtype=np.float32)
        if c == 0:
            xc[W:] = x[0:OWN * W]
        else:
            xc[:] = x[t0 - W:t0 + OWN * W]
        xin = np.ascontiguousarray(xc.T).reshape(NIC, 128, NCI * W)
        prevm = np.full((128, W), MASKV if c == 0 else 0.0, dtype=np.float32)
        in_maps.append({
            "xin": xin, "wqkv": wq_r, "wout": wo_r, "tern": tw_r,
            "bias8": bs_r, "scal": sc, "c_tri": tri, "c_prev": prevm,
            "c_ones_r": ones_r, "c_ones_row": ones_row,
        })
    return in_maps


def kernel(L0_attn_x, L0_attn_w_qkv, L0_attn_w_out, L0_ffn_ternary_weight,
           L0_ffn_bias, L0_ffn_act_scale):
    in_maps = _prep_inputs(L0_attn_x, L0_attn_w_qkv, L0_attn_w_out,
                           L0_ffn_ternary_weight, L0_ffn_bias, L0_ffn_act_scale)
    nc = build_nc(taps=False)
    res = run_bass_kernel_spmd(nc, in_maps, core_ids=list(range(NCORE)))
    out = np.empty((S, D), dtype=np.float32)
    for c in range(NCORE):
        xo = np.asarray(res.results[c]["xout"])
        out[c * OWN * W:(c + 1) * OWN * W] = xo.reshape(D, OWN * W).T
    return out.reshape(1, S, D)
